# version 01 (final)
# baseline (speedup 1.0000x reference)
import contextlib
import numpy as np
import concourse.bass as bass
import concourse.mybir as mybir
from concourse.bass_utils import run_bass_kernel_spmd

F32 = mybir.dt.float32
BF16 = mybir.dt.bfloat16
AF = mybir.ActivationFunctionType
ALU = mybir.AluOpType
AX = mybir.AxisListType

NCORES = 8
L = 2048
D = 2048
MACRO = 512
NMAC = L // MACRO
HALF = 256
NT = MACRO // 128
DSTATE = 16
KER = 4
DTRANK = 128
DIN = 4096
NH = 64
HD = 64
NST = 128
CONVD = DIN + 2 * NST
PROJD = DIN + CONVD + NH
CHUNK = 128
NCH = MACRO // CHUNK
DH = 8192
LNEPS = 1e-5
RMSEPS = 1e-5

ENGS = ("t", "v", "s", "g", "y")
DQS = ("y", "s", "g")

_OP_SIG = {
    "tensor_tensor": (0, (1, 2)), "tensor_scalar": (0, (1, 2, 3)),
    "tensor_copy": (0, (1,)), "scalar_tensor_tensor": (0, (1, 2, 3)),
    "tensor_reduce": (0, (1,)), "reduce_sum": (0, (1,)), "reduce_max": (0, (1,)),
    "memset": (0, ()), "tensor_tensor_scan": (0, (1, 2, 3)),
    "reciprocal": (0, (1,)), "activation": (0, (1,)), "copy": (0, (1,)),
    "select": (0, (1, 2, 3)), "tensor_add": (0, (1, 2)), "tensor_sub": (0, (1, 2)),
    "tensor_mul": (0, (1, 2)), "tensor_max": (0, (1, 2)),
    "iota": (0, ()), "affine_select": (0, (1,)),
}
_HAZ_ENGS = ("v", "s", "g")


def _ap_range(ap):
    try:
        import concourse.mybir as _mb
        esz = _mb.dt.size(ap.dtype)
        off = ap.offset
        span = 0
        dims = ap.ap
        for st_, ct in dims[1:] if len(dims) > 1 else dims:
            span += abs(st_) * (ct - 1)
        fs = 1
        for _, ct in (dims[1:] if len(dims) > 1 else dims):
            fs *= ct
        return (ap.tensor.name, off * esz, (off + span + 1) * esz, fs)
    except Exception:
        return None



class Builder:
    def __init__(self, nc):
        self.nc = nc
        self.stack = contextlib.ExitStack()
        self.dram = {}
        self.q = {e: [] for e in ENGS}
        self.cnt = {e: 0 for e in ENGS}
        self.dcnt = {e: 0 for e in DQS}
        self.last_w = {}
        self.readers = {}
        self.waited = {e: {} for e in ENGS}
        self._eng = None
        self._rd = None
        self._wr = None
        self._haz = {e: [] for e in ENGS}
        self._junk = None

    def sbuf(self, name, shape, dtype):
        import os
        t = self.stack.enter_context(self.nc.sbuf_tensor(name, shape, dtype))
        if os.environ.get("KDBG_SBUF"):
            esz = 2 if dtype == BF16 else 4
            print(f"SBUF {name}: {shape[1]*esz}B rem={self.nc.sbuf_bytes_remaining}")
        return t

    def psum(self, name, shape, dtype):
        return self.stack.enter_context(self.nc.psum_tensor(name, shape, dtype))

    def dram_in(self, name, shape):
        ap = self.nc.dram_tensor(name, list(shape), F32, kind="ExternalInput").ap()
        self.dram[name] = ap
        return ap

    def scratch(self, name, shape, dtype):
        ap = self.nc.dram_tensor(name, list(shape), dtype, kind="Internal").ap()
        self.dram[name] = ap
        return ap

    def dram_out(self, name, shape, dtype=F32):
        ap = self.nc.dram_tensor(name, list(shape), dtype, kind="ExternalOutput").ap()
        self.dram[name] = ap
        return ap

    def _raw_wait(self, eng, tok, self_ok=True):
        key, v = tok
        if key == eng and self_ok:
            return
        got = self.waited[eng].get(key, -1)
        if v <= got:
            return
        self.waited[eng][key] = v
        self.q[eng].append(("wait", tok))

    def _emit_deps(self, eng, reads, writes, is_dma=False):
        self_ok = not is_dma
        for buf in reads:
            t = self.last_w.get(buf)
            if t is not None:
                self._raw_wait(eng, t, self_ok)
        for buf in writes:
            t = self.last_w.get(buf)
            if t is not None:
                self._raw_wait(eng, t, self_ok)
            for t2 in self.readers.get(buf, ()):
                self._raw_wait(eng, t2, self_ok)

    def _finish(self, tok, reads, writes):
        for buf in reads:
            self.readers.setdefault(buf, []).append(tok)
        for buf in writes:
            self.last_w[buf] = tok
            self.readers[buf] = []

    def begin(self, eng, reads=(), writes=()):
        assert self._eng is None
        self._eng, self._rd, self._wr = eng, tuple(reads), tuple(writes)
        self._emit_deps(eng, self._rd, self._wr)

    def op(self, meth, *a):
        kw = a[-1] if a and isinstance(a[-1], dict) else None
        if kw is not None:
            a = a[:-1]
        else:
            kw = {}
        eng = self._eng
        if eng in _HAZ_ENGS and meth in _OP_SIG:
            wi, ris = _OP_SIG[meth]
            reads = []
            for i in ris:
                if i < len(a) and hasattr(a[i], "offset"):
                    reads.append(_ap_range(a[i]))
            for k in ("bias", "scale", "initial"):
                if hasattr(kw.get(k), "offset"):
                    reads.append(_ap_range(kw[k]))
            for dist, (prev_w, kind) in enumerate(self._haz[eng]):
                if kind == "wide":
                    continue
                lim = 3 if kind == "red" else 1
                if dist >= lim:
                    continue
                for r in reads:
                    if r is None:
                        raise RuntimeError(f"hazard-unparseable read in {meth} on {eng}")
                    for wname, wb0, wb1 in prev_w:
                        if r[0] == wname and r[1] < wb1 and wb0 < r[2]:
                            raise RuntimeError(
                                f"RAW hazard: {meth} on '{eng}' reads {r[0]}"
                                f"[{r[1]}:{r[2]}] written {dist+1} op(s) ago by a "
                                f"small producer (kind={kind})")
            ws = []
            kind = "wide"
            if wi < len(a) and hasattr(a[wi], "offset"):
                w = _ap_range(a[wi])
                if w is None:
                    ws, kind = [("?", 0, 1 << 60)], "red"
                else:
                    ws = [w[:3]]
                    if w[3] < 128:
                        kind = "red" if meth in ("tensor_reduce", "reduce_sum", "reduce_max") else "small"
            ao = kw.get("accum_out")
            if hasattr(ao, "offset"):
                w2 = _ap_range(ao)
                ws.append(w2[:3] if w2 else ("?", 0, 1 << 60))
                kind = "red"
            self._haz[eng] = [(ws, kind)] + self._haz[eng][:3]
        elif eng in _HAZ_ENGS:
            self._haz[eng] = [([("?", 0, 1 << 60)], "red")] + self._haz[eng][:3]
        self.q[eng].append(("op", lambda e, m=meth, a=a, kw=kw: getattr(e, m)(*a, **kw)))
        self.q[eng][-1] = self.q[eng][-1] + (meth,)

    def end(self):
        eng = self._eng
        assert self.q[eng] and self.q[eng][-1][0] == "op"
        self.q[eng].append(("inc",))
        self.cnt[eng] += 1
        tok = (eng, self.cnt[eng])
        self._finish(tok, self._rd, self._wr)
        self._eng = None
        return tok

    def stage(self, eng, reads, writes, *ops):
        self.begin(eng, reads, writes)
        for o in ops:
            self.op(*o)
        return self.end()

    def dma(self, q, out, in_, src, dst, transpose=False, slow=False):
        reads = (src,) if isinstance(src, str) else tuple(src or ())
        writes = (dst,) if isinstance(dst, str) else tuple(dst or ())
        self._emit_deps(q, reads, writes, is_dma=True)
        kw = {}
        if transpose:
            kw["transpose"] = True
        if slow:
            kw["allow_slow_non_contiguous"] = True
        self.q[q].append(("op", lambda e, o=out, i=in_, kw=kw: e.dma_start(out=o, in_=i, **kw)))
        self.q[q].append(("dinc",))
        self.dcnt[q] += 16
        tok = ("D" + q, self.dcnt[q])
        self._finish(tok, reads, writes)
        return tok


def build_core_program(dbg=()):
    nc = bass.Bass("TRN2", target_bir_lowering=False)
    b = Builder(nc)
    dbg = set(dbg)

    x_in = b.dram_in("x", (L, D))
    Win = b.dram_in("Win", (2 * D, D))
    cw0_d = b.dram_in("cw0", (D, KER))
    cb0_d = b.dram_in("cb0", (D,))
    Wxp = b.dram_in("Wxp", (DTRANK + 2 * DSTATE, D))
    Wdt = b.dram_in("Wdt", (D, DTRANK))
    bdt_d = b.dram_in("bdt", (D,))
    Alog = b.dram_in("Alog", (D, DSTATE))
    Dsk_d = b.dram_in("Dsk", (D,))
    WoutD = b.dram_in("Wout", (D, D))
    Q0w = b.dram_in("Q0w", (D, D))
    Q0b = b.dram_in("Q0b", (D,))
    PjD = b.dram_in("Pj", (PROJD, D))
    cw1_d = b.dram_in("cw1", (CONVD, KER))
    cb1_d = b.dram_in("cb1", (CONVD,))
    dtb1_d = b.dram_in("dtb1", (NH,))
    Alog1 = b.dram_in("Alog1", (NH,))
    D1_d = b.dram_in("D1", (NH,))
    nw1_d = b.dram_in("nw1", (DIN,))
    OpD = b.dram_in("Op", (D, DIN))
    UpD = b.dram_in("Up", (DH, D))
    DnD = b.dram_in("Dn", (D, DH))
    ups_d = b.dram_in("ups", (DH,))
    dns_d = b.dram_in("dns", (D,))
    oss_d = b.dram_in("oss", (1,))

    out = b.dram_out("out", (L, D))

    sWin = b.scratch("sWin", (2 * D, D), BF16)
    sWout = b.scratch("sWout", (D, D), BF16)
    sQ0 = b.scratch("sQ0", (D, D), BF16)
    sPj = b.scratch("sPj", (PROJD, D), BF16)
    sOp = b.scratch("sOp", (D, DIN), BF16)
    sUp = b.scratch("sUp", (DH, D), BF16)
    sDn = b.scratch("sDn", (D, DH), BF16)
    sWxp = b.scratch("sWxp", (DTRANK + 2 * DSTATE, D), BF16)
    sWdt = b.scratch("sWdt", (D, DTRANK), BF16)
    sX1 = b.scratch("sX1", (L, D), F32)
    sHid = b.scratch("sHid", (DH, MACRO), BF16)
    sBC = b.scratch("sBC", (32, MACRO), F32)
    sAcs = b.scratch("sAcsF", (NH, CHUNK), F32)

    cw0 = b.sbuf("c_cw0", [128, 16 * KER], F32)
    cb0 = b.sbuf("c_cb0", [128, 16], F32)
    bdt = b.sbuf("c_bdt", [128, 16], F32)
    dsk = b.sbuf("c_dsk", [128, 16], F32)
    A0 = b.sbuf("c_A0", [128, 16 * DSTATE], F32)
    cw1 = b.sbuf("c_cw1", [128, 34 * KER], F32)
    cb1 = b.sbuf("c_cb1", [128, 34], F32)
    dtb1 = b.sbuf("c_dtb1", [64, 1], F32)
    A1 = b.sbuf("c_A1", [64, 1], F32)
    D1 = b.sbuf("c_D1", [64, 1], F32)
    repD = b.sbuf("c_repD", [128, NH], F32)
    nw1 = b.sbuf("c_nw1", [128, 32], F32)
    usc = b.sbuf("c_usc", [128, 64], F32)
    b0slab = b.sbuf("c_b0slab", [1, 512], BF16)
    onesb = b.sbuf("c_onesb", [1, 128], BF16)
    onesf = b.sbuf("c_onesf", [1, 128], F32)
    ones64 = b.sbuf("c_ones64", [64, CHUNK], F32)
    identf = b.sbuf("c_identf", [128, 128], F32)
    triu1 = b.sbuf("c_triu1", [128, 128], F32)
    carry = b.sbuf("c_carry", [128, 16 * DSTATE], F32)
    hist0 = b.sbuf("c_hist0", [128, 16 * 3], F32)
    hist1 = b.sbuf("c_hist1", [128, 34 * 3], F32)
    hrun = b.sbuf("c_hrun", [128, NH * HD], F32)
    stat = b.sbuf("c_stat", [128, 8], F32)
    b._junk = stat[:, 7:8]
    dsos = b.sbuf("c_dsos", [128, 16], F32)
    osrep = b.sbuf("c_osrep", [128, 1], F32)
    thr = b.sbuf("c_thr", [128, 2], F32)


    AR_A = b.sbuf("AR_A", [128, 8192], F32)
    AR_B = b.sbuf("AR_B", [128, 8192], F32)
    AR_D = b.sbuf("AR_D", [128, 8192], F32)
    AR_E = b.sbuf("AR_E", [128, 4096], F32)
    AR_F = b.sbuf("AR_F", [128, 4096], F32)
    WBUF0 = b.sbuf("WBUF0", [128, 8192], BF16)
    WBUF1 = b.sbuf("WBUF1", [128, 8192], BF16)
    AR_T = b.sbuf("AR_T", [128, 2048], F32)
    AR_U = b.sbuf("AR_U", [128, 2048], F32)
    SCANT = AR_B[:, 0:2560]
    WXPT = AR_B[:].bitcast(BF16)[:, 0:16 * 160]
    WXPT_N = ("SCdt", "SCw", "SCdA0", "SCdA1")
    WDTT = AR_B[:].bitcast(BF16)[:, 2 * 2560:2 * 2560 + 2048]
    dtrb = AR_B[:].bitcast(BF16)[:, 2 * 3584:2 * 3584 + 512]
    AB_ALL = ("AR_B", "SCdt", "SCw", "SCdA0", "SCdA1", "SCdB0", "SCdB1", "WDTT", "dtrb")
    bcf = AR_U[0:32, 0:512]
    Bb = WBUF1[:].bitcast(F32)
    Cb = WBUF0[:].bitcast(F32)
    Bfm = b.sbuf("t_Bfm", [128, MACRO], BF16)
    Cfm = b.sbuf("t_Cfm", [128, MACRO], BF16)
    dtadt = b.sbuf("t_dtadt", [64, 2 * MACRO], F32)
    dtf = dtadt[:, 0:MACRO]
    adt = dtadt[:, MACRO:2 * MACRO]
    AcsP = adt
    Acst = b.sbuf("t_acst", [128, NH], F32)
    nAcst = b.sbuf("t_nacst", [128, NH], F32)
    eAcst = b.sbuf("t_eacst", [128, NH], F32)
    dtt = b.sbuf("t_dtt", [128, NH], F32)
    dect = b.sbuf("t_dect", [128, NH], F32)
    Bt = b.sbuf("t_Bt", [128, 128], BF16)
    eRc = b.sbuf("t_eRc", [128, NH], F32)
    lastrow = b.sbuf("c_lastrow", [1, 64], F32)
    Bdec8 = WBUF1[:, 0:1024]
    hrunb = WBUF1[:, 1024:1536]

    PB = [b.psum(f"pb{i}", [128, 512], F32) for i in range(8)]
    PBN = [f"PB{i}" for i in range(8)]

    def bfv(t, f32_off, cols):
        return t[:].bitcast(BF16)[:, 2 * f32_off:2 * f32_off + cols]

    XN_FM = bfv(AR_E, 0, 8192)
    XN_ALL = ("XN0", "XN1", "XN2", "XN3")
    U_FM = AR_A
    SZ_FM = bfv(AR_D, 4096, 8192)
    YG_FM = bfv(AR_F, 0, 8192)
    AFN = ("AF_a", "AF_b")
    HSTT = AR_D
    GATE_TM = bfv(AR_B, 0, 16384)
    XFM = bfv(AR_D, 0, 16384)
    ADN = ("AR_Dlo", "AR_Dhi")
    XTM = bfv(AR_A, 0, 16384)
    YNFM = bfv(AR_D, 0, 16384)
    HID = bfv(AR_B, 0, 16384)
    XDT = bfv(AR_T, 0, 4096)
    LEXP = AR_F[:, 0:1024]
    GL8 = bfv(AR_F, 1024, 1024)
    Y8 = AR_F[:, 1536:2048]
    SG8 = AR_F[:, 0:512]
    YF = bfv(AR_F, 2048, 4096)
    WB0N = ("WBUF0a", "WBUF0b")

    SC_dt = SCANT[:, 0:256]
    SC_w = SCANT[:, 256:512]
    SC_y = SCANT[:, 0:256]

    dbg_aps = {}
    if "xn" in dbg:
        dbg_aps["dbg_xn"] = b.dram_out("dbg_xn", (16, 128, L), BF16)
    if "u" in dbg:
        dbg_aps["dbg_u"] = b.dram_out("dbg_u", (16, 128, L))
        dbg_aps["dbg_sz"] = b.dram_out("dbg_sz", (16, 128, L), BF16)
    if "xdbl" in dbg:
        dbg_aps["dbg_dtr"] = b.dram_out("dbg_dtr", (128, L))
        dbg_aps["dbg_bc"] = b.dram_out("dbg_bc", (32, L))
    if "scan" in dbg:
        dbg_aps["dbg_yg"] = b.dram_out("dbg_yg", (16, 128, L), BF16)
    if "x1" in dbg:
        dbg_aps["dbg_x1"] = b.dram_out("dbg_x1", (L, D))
    if "proj" in dbg:
        dbg_aps["dbg_gate"] = b.dram_out("dbg_gate", (L, DIN))
        dbg_aps["dbg_xfm"] = b.dram_out("dbg_xfm", (32, 128, L), BF16)
        dbg_aps["dbg_bfm"] = b.dram_out("dbg_bfm", (NST, L))
        dbg_aps["dbg_cfm"] = b.dram_out("dbg_cfm", (NST, L))
        dbg_aps["dbg_dtf"] = b.dram_out("dbg_dtf", (NH, L))
        dbg_aps["dbg_acs"] = b.dram_out("dbg_acs", (NH, L))
    if "ssd" in dbg:
        dbg_aps["dbg_yn"] = b.dram_out("dbg_yn", (32, 128, L), BF16)
        dbg_aps["dbg_yf"] = b.dram_out("dbg_yf", (L, DIN), BF16)
        dbg_aps["dbg_lex"] = b.dram_out("dbg_lex", (128, 1024), F32)
        dbg_aps["dbg_bcr"] = b.dram_out("dbg_bcr", (128, 2048), F32)
        dbg_aps["dbg_y8"] = b.dram_out("dbg_y8", (128, 512), F32)

    def dump_plane_fm(name, src_bf16, ngroups, t0, tag):
        b.dma("y", dbg_aps[name].rearrange("g q t -> q g t")[:, :, t0:t0 + MACRO],
              src_bf16.rearrange("p (g t) -> p g t", t=512), tag, None)

    b.begin("g", (), ("identf", "triu1", "onesb", "onesf", "ones64", "carry",
                      "hist0", "hist1", "hrun"))
    b.op("memset", identf[:], 1.0)
    b.op("affine_select", identf[:], identf[:], [[-1, 128]], ALU.is_equal, 0.0,
         dict(channel_multiplier=1))
    sl = triu1[:]
    b.op("memset", sl, 1.0)
    b.op("affine_select", sl, sl, [[1, 128]], ALU.is_ge, 0.0, dict(channel_multiplier=-1))
    b.op("memset", onesb[:], 1.0)
    b.op("memset", onesf[:], 1.0)
    b.op("memset", ones64[:], 1.0)
    b.op("memset", carry[:], 0.0)
    b.op("memset", hist0[:], 0.0)
    b.op("memset", hist1[:], 0.0)
    b.op("memset", hrun[:], 0.0)
    b.end()

    b.dma("y", cw0[:].rearrange("p (g k) -> p g k", k=KER),
          cw0_d.rearrange("(g p) k -> p g k", p=128), None, "cw0")
    b.dma("y", cb0[:], cb0_d.rearrange("(g p) -> p g", p=128), None, "cb0", slow=True)
    b.dma("y", bdt[:], bdt_d.rearrange("(g p) -> p g", p=128), None, "bdt", slow=True)
    b.dma("y", dsk[:], Dsk_d.rearrange("(g p) -> p g", p=128), None, "dsk", slow=True)
    b.dma("y", A0[:].rearrange("p (g n) -> p g n", n=DSTATE),
          Alog.rearrange("(g p) n -> p g n", p=128), None, "A0")
    b.dma("y", cw1[:].rearrange("p (g k) -> p g k", k=KER),
          cw1_d.rearrange("(g p) k -> p g k", p=128), None, "cw1")
    b.dma("y", cb1[:], cb1_d.rearrange("(g p) -> p g", p=128), None, "cb1", slow=True)
    b.dma("y", dtb1[:], dtb1_d.unsqueeze(1), None, "dtb1")
    b.dma("y", A1[:], Alog1.unsqueeze(1), None, "A1")
    b.dma("y", D1[:], D1_d.unsqueeze(1), None, "D1")
    b.dma("y", nw1[:], nw1_d.rearrange("(g p) -> p g", p=128), None, "nw1", slow=True)
    b.dma("y", usc[:], ups_d.rearrange("(g p) -> p g", p=128), None, "usc", slow=True)
    b.dma("y", dsos[:], dns_d.rearrange("(g p) -> p g", p=128), None, "dsos", slow=True)
    b.dma("y", osrep[0:1, :], oss_d.unsqueeze(0), None, "osrep")

    b.stage("s", ("A0", "A1"), ("A0", "A1"),
            ("activation", A0[:], A0[:], AF.Exp),
            ("activation", A1[:], A1[:], AF.Exp))
    b.stage("v", ("A0", "A1"), ("A0", "A1"),
            ("tensor_scalar", A0[:], A0[:], -1.0, None, ALU.mult),
            ("tensor_scalar", A1[:], A1[:], -1.0, None, ALU.mult))
    b.stage("t", ("D1", "identf"), ("PB7",),
            ("transpose", PB[7][0:1, 0:64], D1[:], identf[0:64, 0:64]))
    b.stage("v", ("PB7",), ("lastrow",),
            ("tensor_copy", lastrow[:], PB[7][0:1, 0:64]))
    b.stage("t", ("lastrow", "onesf", "osrep"), ("PB6", "PB5"),
            ("matmul", PB[6][:, 0:64], onesf[:], lastrow[:], dict(start=True, stop=True)),
            ("matmul", PB[5][:, 0:1], onesf[:], osrep[0:1, 0:1], dict(start=True, stop=True)))
    b.stage("v", ("PB6", "PB5", "dsos"), ("repD", "osrep", "dsos"),
            ("tensor_copy", osrep[:], PB[5][:, 0:1]),
            ("tensor_copy", repD[:], PB[6][:, 0:64]),
            ("tensor_copy", stat[:, 7:8], osrep[:, 0:1]),
            ("tensor_scalar", dsos[:], dsos[:], osrep[:, 0:1], None, ALU.mult))

    def cast_chunks(dstname, dst, src, rows, cols):
        for i, r0 in enumerate(range(0, rows, 128)):
            rr = min(128, rows - r0)
            buf, nm = (WBUF0, WB0N) if i % 2 == 0 else (WBUF1, ("WBUF1",))
            b.dma("g", buf[0:rr, 0:cols], src[r0:r0 + rr, :], None, nm)
            b.dma("s", dst[r0:r0 + rr, :], buf[0:rr, 0:cols], nm, dstname)

    cast_chunks("sWin", sWin, Win, 2 * D, D)
    cast_chunks("sWxp", sWxp, Wxp, DTRANK + 2 * DSTATE, D)
    cast_chunks("sWdt", sWdt, Wdt, D, DTRANK)
    cast_chunks("sWout", sWout, WoutD, D, D)

    for i in range(D // 128):
        b.dma("g", AR_U[:], Q0w[i * 128:(i + 1) * 128, :], None, "AR_U")
        b.stage("s", ("AR_U",), ("AR_T", "WBUF0a"),
                ("activation", AR_T[:], AR_U[:], AF.Abs),
                ("activation", WBUF0[:, 0:2048], AR_U[:], AF.Sign))
        b.stage("v", ("AR_T",), ("thr",),
                ("tensor_reduce", thr[:, 0:1], AR_T[:], AX.X, ALU.add))
        b.stage("s", ("thr",), ("thr2",),
                ("activation", thr[:, 1:2], thr[:, 0:1], AF.Copy, dict(scale=0.7 / D)))
        b.stage("v", ("AR_T", "thr2", "WBUF0a"), ("AR_T", "WBUF0b"),
                ("tensor_scalar", AR_T[:], AR_T[:], thr[:, 1:2], None, ALU.is_gt),
                ("tensor_tensor", WBUF0[:, 2048:4096], WBUF0[:, 0:2048], AR_T[:], ALU.mult))
        b.dma("g", sQ0[i * 128:(i + 1) * 128, :], WBUF0[:, 2048:4096], "WBUF0b", "sQ0")

    cast_chunks("sPj", sPj, PjD, PROJD, D)
    cast_chunks("sOp", sOp, OpD, D, DIN)

    for i in range(DH // 128):
        b.dma("g", AR_U[:], UpD[i * 128:(i + 1) * 128, :], None, "AR_U")
        b.stage("s", ("AR_U",), ("WBUF0a",),
                ("activation", WBUF0[:, 0:2048], AR_U[:], AF.Sign))
        b.dma("g", sUp[i * 128:(i + 1) * 128, :], WBUF0[:, 0:2048], "WBUF0a", "sUp")

    for i in range(D // 128):
        b.dma("g", AR_A[:], DnD[i * 128:(i + 1) * 128, :], None, "AR_A")
        b.stage("s", ("AR_A",), WB0N,
                ("activation", WBUF0[:], AR_A[:], AF.Sign))
        b.stage("v", WB0N + ("dsos",), WB0N,
                ("tensor_scalar", WBUF0[:], WBUF0[:], dsos[:, i:i + 1], None, ALU.mult))
        b.dma("g", sDn[i * 128:(i + 1) * 128, :], WBUF0[:], WB0N, "sDn")

    PJDT = AR_T[:].bitcast(BF16)[:, 2 * 1024:2 * 1024 + 16 * 128]

    def emit_ln_tile(tt):
        b.stage("v", ("AR_T",), ("statm",),
                ("tensor_reduce", stat[:, 0:1], AR_T[:], AX.X, ALU.add))
        b.stage("s", ("statm",), ("statmn",),
                ("activation", stat[:, 1:2], stat[:, 0:1], AF.Copy, dict(scale=1.0 / D)))
        b.stage("v", ("statmn", "AR_T"), ("AR_U",),
                ("tensor_scalar", AR_U[:], AR_T[:], stat[:, 1:2], None, ALU.subtract))
        b.stage("s", ("AR_U",), ("AR_T", "statq"),
                ("activation", AR_T[:], AR_U[:], AF.Square, dict(accum_out=stat[:, 2:3])))
        b.stage("v", ("statq",), ("statv",),
                ("tensor_scalar", stat[:, 3:4], stat[:, 2:3], 1.0 / D, LNEPS, ALU.mult, ALU.add))
        b.stage("s", ("statv",), ("statr1",),
                ("activation", stat[:, 4:5], stat[:, 3:4], AF.Sqrt))
        b.stage("v", ("statr1",), ("statr",),
                ("reciprocal", stat[:, 5:6], stat[:, 4:5]))
        b.stage("s", ("statr", "AR_U"), ("AR_T",),
                ("activation", bfv(AR_T, 0, 2048), AR_U[:], AF.Copy, dict(scale=stat[:, 5:6])))
        b.dma("y", XN_FM.rearrange("p (g t) -> p g t", t=512)[:, :, tt * 128:(tt + 1) * 128],
              bfv(AR_T, 0, 2048), "AR_T", "XN" + str(tt), transpose=True)

    for m in range(NMAC):
        t0 = m * MACRO

        for tt in range(NT):
            b.dma("s", AR_T[:], x_in[t0 + tt * 128: t0 + (tt + 1) * 128, :], None, "AR_T")
            emit_ln_tile(tt)

        if "xn" in dbg:
            dump_plane_fm("dbg_xn", XN_FM, 16, t0, XN_ALL)

        for oc in range(8):
            wbn = WB0N if oc % 2 == 0 else ("WBUF1",)
            wb = WBUF0 if oc % 2 == 0 else WBUF1
            qq = "y"
            b.dma(qq, wb[:].rearrange("p (g o) -> p g o", o=512),
                  sWin[oc * 512:(oc + 1) * 512, :], "sWin", wbn, transpose=True)
            for og in range(4):
                fg = oc * 4 + og
                bank = PB[og]
                b.begin("t", reads=wbn + XN_ALL, writes=(PBN[og],))
                for ig in range(16):
                    b.op("matmul", bank[:, 0:512],
                         wb[:, ig * 512 + og * 128: ig * 512 + og * 128 + 128],
                         XN_FM[:, ig * 512:(ig + 1) * 512],
                         dict(start=(ig == 0), stop=(ig == 15)))
                b.end()
                if fg < 16:
                    b.stage("v", (PBN[og], "hist0", "cw0"), ("AR_U", "hist0"),
                            ("tensor_copy", AR_U[:, 3:515], bank[:, 0:512]),
                            ("tensor_copy", AR_U[:, 0:3], hist0[:, fg * 3:fg * 3 + 3]),
                            ("tensor_copy", hist0[:, fg * 3:fg * 3 + 3], AR_U[:, 512:515]),
                            ("tensor_scalar", AR_U[:, 515:1027], AR_U[:, 0:512],
                             cw0[:, fg * KER:fg * KER + 1], None, ALU.mult),
                            *[("scalar_tensor_tensor", AR_U[:, 515:1027], AR_U[:, k:k + 512],
                               cw0[:, fg * KER + k:fg * KER + k + 1], AR_U[:, 515:1027],
                               ALU.mult, ALU.add) for k in range(1, KER)])
                    b.stage("s", ("AR_U", "cb0"), ("AR_A",) + AFN,
                            ("activation", U_FM[:, fg * 512:(fg + 1) * 512], AR_U[:, 515:1027],
                             AF.Silu, dict(bias=cb0[:, fg:fg + 1])),
                            ("copy", YG_FM[:, fg * 512:(fg + 1) * 512],
                             U_FM[:, fg * 512:(fg + 1) * 512]))
                else:
                    zg = fg - 16
                    b.stage("v", (PBN[og],), ("AR_U",),
                            ("tensor_copy", AR_U[:, 0:512], bank[:, 0:512]))
                    b.stage("s", ("AR_U",), ("AR_Dhi",),
                            ("activation", SZ_FM[:, zg * 512:(zg + 1) * 512],
                             AR_U[:, 0:512], AF.Silu))

        if "u" in dbg:
            b.dma("y", dbg_aps["dbg_u"].rearrange("g q t -> q g t")[:, :, t0:t0 + MACRO],
                  U_FM[:].rearrange("p (g t) -> p g t", t=512), "AR_A", None)
            dump_plane_fm("dbg_sz", SZ_FM, 16, t0, ("AR_Dhi",))
            if "donly" in dbg:
                continue

        b.dma("y", WXPT.rearrange("p (g o) -> p g o", o=160)[:, :, 0:128],
              sWxp[0:128, :], "sWxp", WXPT_N, transpose=True)
        b.dma("y", WXPT.rearrange("p (g o) -> p g o", o=160)[:, :, 32:160],
              sWxp[32:160, :], "sWxp", WXPT_N, transpose=True)
        b.begin("t", reads=WXPT_N + AFN, writes=("PB4",))
        for ig in range(16):
            b.op("matmul", PB[4][:, 0:512], WXPT[:, ig * 160:ig * 160 + 128],
                 YG_FM[:, ig * 512:(ig + 1) * 512], dict(start=(ig == 0), stop=(ig == 15)))
        b.end()
        b.begin("t", reads=WXPT_N + AFN, writes=("PB5",))
        for ig in range(16):
            b.op("matmul", PB[5][0:32, 0:512], WXPT[:, ig * 160 + 128:ig * 160 + 160],
                 YG_FM[:, ig * 512:(ig + 1) * 512], dict(start=(ig == 0), stop=(ig == 15)))
        b.end()
        b.stage("s", ("PB4",), ("dtrb",), ("copy", dtrb[:], PB[4][:, 0:512]))
        b.stage("v", ("PB5",), ("AR_U",), ("tensor_copy", bcf, PB[5][0:32, 0:512]))
        b.dma("g", sBC[:, :], bcf, "AR_U", "sBC")

        if "xdbl" in dbg:
            b.dma("y", dbg_aps["dbg_bc"][:, t0:t0 + MACRO], bcf, "AR_U", None)
            b.stage("s", ("dtrb",), ("AR_U2",), ("copy", AR_U[:, 512:1024], dtrb))
            b.dma("y", dbg_aps["dbg_dtr"][:, t0:t0 + MACRO], AR_U[:, 512:1024], "AR_U2", None)

        for sl in range(4):
            b.dma("y", WDTT[:, sl * 512:(sl + 1) * 512],
                  sWdt[sl * 512:(sl + 1) * 512, :], "sWdt", "WDTT", transpose=True)
        for hf in range(2):
            hoff = hf * HALF
            b.dma("g", Bb.rearrange("p (n t) -> p n t", t=HALF),
                  sBC.unsqueeze(0)[0:1, 0:16, hoff:hoff + HALF].to_broadcast([128, 16, HALF]),
                  "sBC", "WBUF1")
            b.dma("g", Cb.rearrange("p (n t) -> p n t", t=HALF),
                  sBC.unsqueeze(0)[0:1, 16:32, hoff:hoff + HALF].to_broadcast([128, 16, HALF]),
                  "sBC", WB0N)
            for g in range(16):
                b.stage("t", ("WDTT", "dtrb"), ("PB6",),
                        ("matmul", PB[6][:, 0:HALF], WDTT[:, g * 128:(g + 1) * 128],
                         dtrb[:, hoff:hoff + HALF], dict(start=True, stop=True)))
                b.stage("s", ("PB6", "bdt"), ("SCdt",),
                        ("activation", SC_dt, PB[6][:, 0:HALF], AF.Exp,
                         dict(bias=bdt[:, g:g + 1])),
                        ("activation", SC_dt, SC_dt, AF.Ln, dict(bias=1.0)))
                b.stage("g", ("SCdt", "AR_A"), ("SCw",),
                        ("tensor_tensor", SC_w, SC_dt,
                         U_FM[:, g * 512 + hoff:g * 512 + hoff + HALF], ALU.mult))
                for np_ in range(8):
                    bk = np_ % 2
                    n0 = np_ * 2
                    dA = SCANT[:, 512 + bk * 512: 512 + bk * 512 + 512]
                    dB = SCANT[:, 1536 + bk * 512: 1536 + bk * 512 + 512]
                    b.stage("s", ("SCdt", "A0"), (f"SCdA{bk}",),
                            ("activation", dA[:, 0:HALF], SC_dt, AF.Exp,
                             dict(scale=A0[:, g * DSTATE + n0:g * DSTATE + n0 + 1])),
                            ("activation", dA[:, HALF:2 * HALF], SC_dt, AF.Exp,
                             dict(scale=A0[:, g * DSTATE + n0 + 1:g * DSTATE + n0 + 2])))
                    b.stage("g", ("SCw", "WBUF1"), (f"SCdB{bk}",),
                            ("tensor_tensor", dB[:, 0:HALF], SC_w,
                             Bb[:, n0 * HALF:(n0 + 1) * HALF], ALU.mult),
                            ("tensor_tensor", dB[:, HALF:2 * HALF], SC_w,
                             Bb[:, (n0 + 1) * HALF:(n0 + 2) * HALF], ALU.mult))
                    b.stage("v", (f"SCdA{bk}", f"SCdB{bk}", "carry"), ("AR_Dlo", "carry"),
                            ("tensor_tensor_scan", HSTT[:, n0 * HALF:(n0 + 1) * HALF],
                             dA[:, 0:HALF], dB[:, 0:HALF],
                             carry[:, g * DSTATE + n0:g * DSTATE + n0 + 1], ALU.mult, ALU.add),
                            ("tensor_tensor_scan", HSTT[:, (n0 + 1) * HALF:(n0 + 2) * HALF],
                             dA[:, HALF:2 * HALF], dB[:, HALF:2 * HALF],
                             carry[:, g * DSTATE + n0 + 1:g * DSTATE + n0 + 2], ALU.mult, ALU.add))
                b.stage("v", ("AR_Dlo",), ("carry",),
                        ("tensor_copy", carry[:, g * DSTATE:(g + 1) * DSTATE],
                         HSTT[:].rearrange("p (n t) -> p n t", t=HALF)[:, 0:DSTATE, HALF - 1:HALF].rearrange("p n o -> p (n o)")))
                b.stage("g", WB0N + ("AR_Dlo", "carry"), ("AR_Dlo",),
                        ("tensor_tensor", HSTT[:, 0:DSTATE * HALF], HSTT[:, 0:DSTATE * HALF],
                         Cb[:, 0:DSTATE * HALF], ALU.mult))
                b.stage("v", ("AR_Dlo",), ("SCdt",),
                        ("tensor_reduce", SC_y,
                         HSTT[:].rearrange("p (n t) -> p t n", t=HALF)[:, :, 0:DSTATE],
                         AX.X, ALU.add))
                b.stage("v", ("SCdt", "AR_A", "dsk", "AR_Dhi"), AFN + ("SCdt",),
                        ("scalar_tensor_tensor", SC_y,
                         U_FM[:, g * 512 + hoff:g * 512 + hoff + HALF], dsk[:, g:g + 1],
                         SC_y, ALU.mult, ALU.add),
                        ("tensor_tensor", YG_FM[:, g * 512 + hoff:g * 512 + hoff + HALF],
                         SC_y, SZ_FM[:, g * 512 + hoff:g * 512 + hoff + HALF], ALU.mult))

        if "scan" in dbg:
            dump_plane_fm("dbg_yg", YG_FM, 16, t0, AFN)

        for os_ in range(4):
            b.dma("y", WBUF0[:].rearrange("p (g o) -> p g o", o=512),
                  sWout[os_ * 512:(os_ + 1) * 512, :], "sWout", WB0N, transpose=True)
            b.dma("y", WBUF1[:].rearrange("p (g o) -> p g o", o=512),
                  sQ0[os_ * 512:(os_ + 1) * 512, :], "sQ0", "WBUF1", transpose=True)
            b.dma("g", b0slab[:], Q0b.unsqueeze(0)[0:1, os_ * 512:(os_ + 1) * 512],
                  None, "b0slab")
            for tt in range(NT):
                bank = PB[tt]
                b.begin("t", reads=WB0N + ("WBUF1", "b0slab", "onesb") + AFN + XN_ALL,
                        writes=(PBN[tt],))
                for ig in range(16):
                    b.op("matmul", bank[:, 0:512],
                         YG_FM[:, ig * 512 + tt * 128: ig * 512 + tt * 128 + 128],
                         WBUF0[:, ig * 512:(ig + 1) * 512],
                         dict(start=(ig == 0), stop=False))
                for ig in range(16):
                    b.op("matmul", bank[:, 0:512],
                         XN_FM[:, ig * 512 + tt * 128: ig * 512 + tt * 128 + 128],
                         WBUF1[:, ig * 512:(ig + 1) * 512],
                         dict(start=False, stop=False))
                b.op("matmul", bank[:, 0:512], onesb[0:1, 0:128],
                     b0slab[0:1, 0:512], dict(start=False, stop=True))
                b.end()
                b.dma("s", AR_T[:, 0:512],
                      x_in[t0 + tt * 128: t0 + (tt + 1) * 128, os_ * 512:(os_ + 1) * 512],
                      None, "AR_T")
                b.stage("v", (PBN[tt], "AR_T"), ("AR_U",),
                        ("tensor_tensor", AR_U[:, 0:512], bank[:, 0:512], AR_T[:, 0:512],
                         ALU.add))
                b.dma("y", sX1[t0 + tt * 128: t0 + (tt + 1) * 128, os_ * 512:(os_ + 1) * 512],
                      AR_U[:, 0:512], "AR_U", "sX1")
                if "x1" in dbg:
                    b.dma("y", dbg_aps["dbg_x1"][t0 + tt * 128: t0 + (tt + 1) * 128,
                                                 os_ * 512:(os_ + 1) * 512],
                          AR_U[:, 0:512], "AR_U", None)

        if dbg and not (dbg & {"proj", "ssd", "l1"}):
            continue

        for tt in range(NT):
            b.dma("s", AR_T[:], sX1[t0 + tt * 128: t0 + (tt + 1) * 128, :], "sX1", "AR_T")
            emit_ln_tile(tt)

        for osl in range(8):
            wbn = WB0N if osl % 2 == 0 else ("WBUF1",)
            wb = WBUF0 if osl % 2 == 0 else WBUF1
            qq = "y"
            b.dma(qq, wb[:].rearrange("p (g o) -> p g o", o=512),
                  sPj[osl * 512:(osl + 1) * 512, :], "sPj", wbn, transpose=True)
            for tt in range(NT):
                bank = PB[tt]
                b.begin("t", reads=wbn + XN_ALL, writes=(PBN[tt],))
                for ig in range(16):
                    b.op("matmul", bank[:, 0:512],
                         XN_FM[:, ig * 512 + tt * 128: ig * 512 + tt * 128 + 128],
                         wb[:, ig * 512:(ig + 1) * 512], dict(start=(ig == 0), stop=(ig == 15)))
                b.end()
                b.stage("s", (PBN[tt],), AB_ALL,
                        ("copy", GATE_TM[:, tt * 4096 + osl * 512: tt * 4096 + (osl + 1) * 512],
                         bank[:, 0:512]))
        for isl in range(9):
            rows = 512 if isl < 8 else 256
            wbn = WB0N if isl % 2 == 0 else ("WBUF1",)
            wb = WBUF0 if isl % 2 == 0 else WBUF1
            qq = "y"
            r0 = DIN + isl * 512
            dst_ap = wb[:].rearrange("p (g o) -> p g o", o=512)
            if rows < 512:
                dst_ap = dst_ap[:, :, 0:rows]
            b.dma(qq, dst_ap, sPj[r0:r0 + rows, :], "sPj", wbn, transpose=True)
            for og in range(rows // 128):
                fc = isl * 4 + og
                bank = PB[4 + (og % 2)]
                b.begin("t", reads=wbn + XN_ALL, writes=(PBN[4 + og % 2],))
                for ig in range(16):
                    b.op("matmul", bank[:, 0:512],
                         wb[:, ig * 512 + og * 128: ig * 512 + og * 128 + 128],
                         XN_FM[:, ig * 512:(ig + 1) * 512], dict(start=(ig == 0), stop=(ig == 15)))
                b.end()
                b.stage("v", (PBN[4 + og % 2], "hist1", "cw1"), ("AR_U", "hist1"),
                        ("tensor_copy", AR_U[:, 3:515], bank[:, 0:512]),
                        ("tensor_copy", AR_U[:, 0:3], hist1[:, fc * 3:fc * 3 + 3]),
                        ("tensor_copy", hist1[:, fc * 3:fc * 3 + 3], AR_U[:, 512:515]),
                        ("tensor_scalar", AR_U[:, 515:1027], AR_U[:, 0:512],
                         cw1[:, fc * KER:fc * KER + 1], None, ALU.mult),
                        *[("scalar_tensor_tensor", AR_U[:, 515:1027], AR_U[:, k:k + 512],
                           cw1[:, fc * KER + k:fc * KER + k + 1], AR_U[:, 515:1027],
                           ALU.mult, ALU.add) for k in range(1, KER)])
                if fc < 32:
                    dst, dname = XFM[:, fc * 512:(fc + 1) * 512], ADN
                elif fc == 32:
                    dst, dname = Bfm[:], ("Bfm",)
                else:
                    dst, dname = Cfm[:], ("Cfm",)
                b.stage("s", ("AR_U", "cb1"), dname,
                        ("activation", dst, AR_U[:, 515:1027], AF.Silu,
                         dict(bias=cb1[:, fc:fc + 1])))
        b.dma("y", PJDT.rearrange("p (g o) -> p g o", o=128),
              sPj[PROJD - 128:PROJD, :], "sPj", "AR_T", transpose=True)
        b.begin("t", reads=("AR_T",) + XN_ALL, writes=("PB6",))
        for ig in range(16):
            b.op("matmul", PB[6][0:64, 0:512], PJDT[:, ig * 128 + 64:(ig + 1) * 128],
                 XN_FM[:, ig * 512:(ig + 1) * 512], dict(start=(ig == 0), stop=(ig == 15)))
        b.end()
        b.stage("s", ("PB6", "dtb1"), ("dtf",),
                ("activation", dtf, PB[6][0:64, 0:512], AF.Exp, dict(bias=dtb1[:, 0:1])),
                ("activation", dtf, dtf, AF.Ln, dict(bias=1.0)))
        b.begin("v", reads=("dtf", "A1", "ones64"), writes=("adt",))
        b.op("tensor_scalar", adt, dtf, A1[:, 0:1], None, ALU.mult)
        for c in range(NCH):
            b.op("tensor_tensor_scan", AcsP[:, c * 128:(c + 1) * 128], ones64[:],
                 adt[:, c * 128:(c + 1) * 128], 0.0, ALU.mult, ALU.add)
        b.end()

        if "proj" in dbg:
            for tt in range(NT):
                b.stage("s", ("AR_B",), ("AR_U",),
                        ("copy", AR_U[:], GATE_TM[:, tt * 4096: tt * 4096 + 2048]))
                b.dma("y", dbg_aps["dbg_gate"][t0 + tt * 128:t0 + (tt + 1) * 128, 0:2048],
                      AR_U[:], "AR_U", None)
                b.stage("s", ("AR_B",), ("AR_U",),
                        ("copy", AR_U[:], GATE_TM[:, tt * 4096 + 2048: tt * 4096 + 4096]))
                b.dma("y", dbg_aps["dbg_gate"][t0 + tt * 128:t0 + (tt + 1) * 128, 2048:4096],
                      AR_U[:], "AR_U", None)
            dump_plane_fm("dbg_xfm", XFM, 32, t0, ADN)
            b.stage("s", ("Bfm",), ("AR_U",), ("copy", AR_U[:, 0:512], Bfm[:]))
            b.stage("s", ("Cfm",), ("AR_U",), ("copy", AR_U[:, 512:1024], Cfm[:]))
            b.dma("y", dbg_aps["dbg_bfm"][:, t0:t0 + MACRO], AR_U[:, 0:512], "AR_U", None)
            b.dma("y", dbg_aps["dbg_cfm"][:, t0:t0 + MACRO], AR_U[:, 512:1024], "AR_U", None)
            b.dma("y", dbg_aps["dbg_dtf"][:, t0:t0 + MACRO], dtf, "dtf", None)
            b.dma("y", dbg_aps["dbg_acs"][:, t0:t0 + MACRO], AcsP, "adt", None)
            continue

        for gg in range(32):
            b.dma("y", XTM.rearrange("p (c d) -> p c d", d=4096)[:, :, gg * 128:(gg + 1) * 128],
                  XFM[:, gg * 512:(gg + 1) * 512], ADN, "AR_A", transpose=True)

        for c in range(NCH):
            ch0 = c * 128
            b.dma("g", sAcs[:, :], AcsP[:, ch0:ch0 + 128], "adt", "sAcs")
            b.stage("t", ("adt", "dtf", "identf"), ("PB7",),
                    ("transpose", PB[7][:, 0:64], AcsP[:, ch0:ch0 + 128], identf[0:64, 0:64]),
                    ("transpose", PB[7][:, 64:128], dtf[:, ch0:ch0 + 128], identf[0:64, 0:64]))
            b.stage("v", ("PB7",), ("Acst", "nAcst", "dtt"),
                    ("tensor_copy", Acst[:], PB[7][:, 0:64]),
                    ("tensor_scalar", nAcst[:], PB[7][:, 0:64], -1.0, None, ALU.mult),
                    ("tensor_copy", dtt[:], PB[7][:, 64:128]))
            b.stage("s", ("Acst",), ("eAcst",),
                    ("activation", eAcst[:], Acst[:], AF.Exp))
            b.stage("t", ("adt", "identf"), ("PB7",),
                    ("transpose", PB[7][0:1, 128:192], AcsP[:, ch0 + 127:ch0 + 128],
                     identf[0:64, 0:64]))
            b.stage("v", ("PB7",), ("lastrow",),
                    ("tensor_copy", lastrow[:], PB[7][0:1, 128:192]))
            b.stage("t", ("lastrow", "onesf"), ("PB6",),
                    ("matmul", PB[6][:, 0:64], onesf[:], lastrow[:],
                     dict(start=True, stop=True)))
            b.stage("v", ("PB6", "Acst"), ("dect",),
                    ("tensor_tensor", dect[:], PB[6][:, 0:64], Acst[:], ALU.subtract))
            b.stage("s", ("dect", "PB6"), ("dect", "eRc"),
                    ("activation", dect[:], dect[:], AF.Exp),
                    ("activation", eRc[:], PB[6][:, 0:64], AF.Exp))
            b.dma("y", Bt[:], Bfm[:, ch0:ch0 + 128], "Bfm", "Bt", transpose=True)
            b.stage("t", ("Bfm", "Cfm"), ("PB5",),
                    ("matmul", PB[5][:, 0:128], Bfm[:, ch0:ch0 + 128], Cfm[:, ch0:ch0 + 128],
                     dict(start=True, stop=True)))
            b.begin("v", reads=("AR_A", "dtt"), writes=("AR_T",))
            for h in range(NH):
                b.op("tensor_scalar", XDT[:, h * HD:(h + 1) * HD],
                     XTM[:, c * DIN + h * HD: c * DIN + (h + 1) * HD],
                     dtt[:, h:h + 1], None, ALU.mult)
            b.end()

            for hb in range(8):
                h0 = hb * 8
                pby, pbyn = PB[hb % 2], PBN[hb % 2]
                pbo, pbon = PB[2 + hb % 2], PBN[2 + hb % 2]
                pbs, pbsn = PB[4], PBN[4]
                if hb % 2 == 0:
                    qd = hb // 2
                    b.dma("g", AR_U[:],
                          sAcs.rearrange("h t -> (h t)").unsqueeze(0)[0:1, qd * 2048:(qd + 1) * 2048].to_broadcast([128, 2048]),
                          "sAcs", "AR_U")
                if "ssd" in dbg and m == 0 and c == 0 and hb == 0:
                    b.dma("y", dbg_aps["dbg_bcr"], AR_U[:], "AR_U", None)
                b.begin("v", reads=("AR_U", "nAcst"), writes=("AF_a",))
                for hh in range(8):
                    h = h0 + hh
                    b.op("tensor_scalar", LEXP[:, hh * 128:(hh + 1) * 128],
                         AR_U[:, (h % 16) * 128:(h % 16 + 1) * 128],
                         nAcst[:, h:h + 1], 0.0, ALU.add, ALU.min)
                b.end()
                b.begin("s", reads=("AF_a",), writes=("AF_a",))
                for hh in range(8):
                    b.op("activation", LEXP[:, hh * 128:(hh + 1) * 128],
                         LEXP[:, hh * 128:(hh + 1) * 128], AF.Exp)
                b.end()
                if "ssd" in dbg and m == 0 and c == 0 and hb == 0:
                    b.dma("y", dbg_aps["dbg_lex"], LEXP, "AF_a", None)
                b.begin("v", reads=("AF_a", "PB5", "triu1"), writes=("AF_a",))
                for hh in range(8):
                    b.op("tensor_tensor", LEXP[:, hh * 128:(hh + 1) * 128],
                         LEXP[:, hh * 128:(hh + 1) * 128], PB[5][:, 0:128], ALU.mult)
                for hh in range(8):
                    b.op("tensor_tensor", GL8[:, hh * 128:(hh + 1) * 128],
                         LEXP[:, hh * 128:(hh + 1) * 128], triu1[:], ALU.mult)
                b.end()
                b.stage("v", ("hrun",), ("WBUF1",),
                        ("tensor_copy", hrunb, hrun[:, h0 * HD:(h0 + 8) * HD]))
                b.begin("t", reads=("AF_a", "AR_T", "Cfm", "WBUF1"), writes=(pbyn, pbon))
                for hh in range(8):
                    h = h0 + hh
                    b.op("matmul", pby[:, hh * HD:(hh + 1) * HD], GL8[:, hh * 128:(hh + 1) * 128],
                         XDT[:, h * HD:(h + 1) * HD], dict(start=(hh == 0), stop=(hh == 7)))
                b.op("matmul", pbo[:, 0:512], Cfm[:, ch0:ch0 + 128],
                     hrunb, dict(start=True, stop=True))
                b.end()
                b.begin("v", reads=("Bt", "dect"), writes=("WBUF1",))
                for hh in range(8):
                    h = h0 + hh
                    b.op("tensor_scalar", Bdec8[:, hh * 128:(hh + 1) * 128], Bt[:],
                         dect[:, h:h + 1], None, ALU.mult)
                b.end()
                b.begin("t", reads=("WBUF1", "AR_T"), writes=(pbsn,))
                for hh in range(8):
                    h = h0 + hh
                    b.op("matmul", pbs[:, hh * HD:(hh + 1) * HD],
                         Bdec8[:, hh * 128:(hh + 1) * 128], XDT[:, h * HD:(h + 1) * HD],
                         dict(start=(hh == 0), stop=(hh == 7)))
                b.end()
                b.begin("v", reads=(pbyn, pbon, "eAcst", "AR_A", "repD"), writes=("AF_ay",))
                for hh in range(8):
                    h = h0 + hh
                    b.op("tensor_scalar", Y8[:, hh * HD:(hh + 1) * HD],
                         pbo[:, hh * HD:(hh + 1) * HD], eAcst[:, h:h + 1], None, ALU.mult)
                for hh in range(8):
                    h = h0 + hh
                    b.op("scalar_tensor_tensor", Y8[:, hh * HD:(hh + 1) * HD],
                         XTM[:, c * DIN + h * HD: c * DIN + (h + 1) * HD],
                         repD[:, h:h + 1], Y8[:, hh * HD:(hh + 1) * HD], ALU.mult, ALU.add)
                b.op("tensor_tensor", Y8[:, 0:448], Y8[:, 0:448], pby[:, 0:448], ALU.add)
                b.op("tensor_tensor", Y8[:, 448:512], Y8[:, 448:512], pby[:, 448:512], ALU.add)
                b.end()
                b.stage("s", ("AR_B",), ("AF_a",),
                        ("activation", SG8, GATE_TM[:, c * DIN + h0 * HD: c * DIN + (h0 + 8) * HD],
                         AF.Silu))
                b.stage("v", ("AF_ay", "AF_a"), ("AF_b",),
                        ("tensor_tensor", YF[:, h0 * HD:h0 * HD + 448], Y8[:, 0:448],
                         SG8[:, 0:448], ALU.mult),
                        ("tensor_tensor", YF[:, h0 * HD + 448:(h0 + 8) * HD],
                         Y8[:, 448:512], SG8[:, 448:512], ALU.mult))
                b.begin("v", reads=(pbsn, "eRc"), writes=("hrun",))
                for hh in range(8):
                    h = h0 + hh
                    b.op("scalar_tensor_tensor", hrun[:, h * HD:(h + 1) * HD],
                         hrun[:, h * HD:(h + 1) * HD], eRc[:, h:h + 1],
                         pbs[:, hh * HD:(hh + 1) * HD], ALU.mult, ALU.add)
                b.end()

            if "ssd" in dbg:
                b.dma("y", dbg_aps["dbg_yf"][t0 + ch0:t0 + ch0 + 128, :], YF, "AF_b", None)
                if m == 0 and c == 0:
                    b.stage("v", ("AF_ay",), ("AR_U",),
                            ("tensor_copy", AR_U[:, 0:512], Y8))
                    b.dma("y", dbg_aps["dbg_y8"], AR_U[:, 0:512], "AR_U", None)
            b.stage("s", ("AF_b",), ("AR_T", "statq"),
                    ("activation", bfv(AR_T, 0, 4096), YF, AF.Square,
                     dict(accum_out=stat[:, 2:3])))
            b.stage("v", ("statq",), ("statv",),
                    ("tensor_scalar", stat[:, 3:4], stat[:, 2:3], 1.0 / DIN, RMSEPS,
                     ALU.mult, ALU.add))
            b.stage("s", ("statv",), ("statr1",),
                    ("activation", stat[:, 4:5], stat[:, 3:4], AF.Sqrt))
            b.stage("v", ("statr1",), ("statr",),
                    ("reciprocal", stat[:, 5:6], stat[:, 4:5]))
            b.stage("s", ("statr", "AF_b"), ("AR_T",),
                    ("activation", bfv(AR_T, 0, 4096), YF, AF.Copy, dict(scale=stat[:, 5:6])))
            b.dma("y", YNFM.rearrange("p (g t) -> p g t", t=512)[:, :, ch0:ch0 + 128],
                  bfv(AR_T, 0, 4096), "AR_T", ADN, transpose=True)

        b.begin("v", reads=ADN + ("nw1",), writes=ADN)
        for gg in range(32):
            b.op("tensor_scalar", YNFM[:, gg * 512:(gg + 1) * 512],
                 YNFM[:, gg * 512:(gg + 1) * 512], nw1[:, gg:gg + 1], None, ALU.mult)
        b.end()

        if "ssd" in dbg:
            dump_plane_fm("dbg_yn", YNFM, 32, t0, ADN)
            continue

        HIDS = AR_T[:].bitcast(BF16)
        for hs in range(16):
            b.dma("y", WBUF0[:].rearrange("p (g o) -> p g o", o=512),
                  sUp[hs * 512:(hs + 1) * 512, :], "sUp", WB0N, transpose=True)
            hbn = ("AR_Ta", "AR_T") if hs % 2 == 0 else ("AR_Tb", "AR_T")
            hoff = 0 if hs % 2 == 0 else 2048
            for hc in range(4):
                hcg = hs * 4 + hc
                b.begin("t", reads=WB0N + XN_ALL, writes=("PB6",))
                for ig in range(16):
                    b.op("matmul", PB[6][:, 0:512],
                         WBUF0[:, ig * 512 + hc * 128: ig * 512 + hc * 128 + 128],
                         XN_FM[:, ig * 512:(ig + 1) * 512],
                         dict(start=(ig == 0), stop=(ig == 15)))
                b.end()
                b.stage("s", ("PB6", "usc"), hbn,
                        ("activation", HIDS[:, hoff + hc * 512: hoff + (hc + 1) * 512],
                         PB[6][:, 0:512], AF.Relu, dict(scale=usc[:, hcg:hcg + 1])))
            b.dma("g", sHid[hs * 512:(hs + 1) * 512, :].rearrange("(hcl p) t -> p hcl t", p=128),
                  HIDS[:, hoff:hoff + 2048].rearrange("p (hcl t) -> p hcl t", t=512),
                  hbn, "sHid")
        for osh in range(2):
            for os2 in range(2):
                os_ = osh * 2 + os2
                for ih in range(2):
                    b.dma("y", WBUF1[:].rearrange("p (g o) -> p g o", o=512),
                          sOp[os_ * 512:(os_ + 1) * 512, ih * 2048:(ih + 1) * 2048],
                          "sOp", "WBUF1", transpose=True)
                    for tt in range(NT):
                        bank = PB[os2 * 4 + tt]
                        b.begin("t", reads=("WBUF1",) + ADN, writes=(PBN[os2 * 4 + tt],))
                        for ig in range(16):
                            b.op("matmul", bank[:, 0:512],
                                 YNFM[:, (ih * 16 + ig) * 512 + tt * 128:
                                      (ih * 16 + ig) * 512 + tt * 128 + 128],
                                 WBUF1[:, ig * 512:(ig + 1) * 512],
                                 dict(start=(ih == 0 and ig == 0), stop=False))
                        b.end()
            for hcg4 in range(16):
                hbn = ("AR_Ta", "AR_T") if hcg4 % 2 == 0 else ("AR_Tb", "AR_T")
                hoff = 0 if hcg4 % 2 == 0 else 2048
                b.dma("g", HIDS[:, hoff:hoff + 2048].rearrange("p (hcl t) -> p hcl t", t=512),
                      sHid[hcg4 * 512:(hcg4 + 1) * 512, :].rearrange("(hcl p) t -> p hcl t", p=128),
                      "sHid", hbn)
                b.dma("y", WBUF1[:, 0:4096].rearrange("p (g o) -> p g o", o=1024),
                      sDn[osh * 1024:(osh + 1) * 1024, hcg4 * 512:(hcg4 + 1) * 512],
                      "sDn", "WBUF1", transpose=True)
                b.begin("t", reads=("WBUF1",) + hbn,
                        writes=tuple(PBN[i] for i in range(8)))
                for hcl in range(4):
                    for os2 in range(2):
                        for tt in range(NT):
                            last = (hcg4 == 15 and hcl == 3)
                            b.op("matmul", PB[os2 * 4 + tt][:, 0:512],
                                 HIDS[:, hoff + hcl * 512 + tt * 128: hoff + hcl * 512 + tt * 128 + 128],
                                 WBUF1[:, hcl * 1024 + os2 * 512: hcl * 1024 + (os2 + 1) * 512],
                                 dict(start=False, stop=last))
                b.end()
            for os2 in range(2):
                os_ = osh * 2 + os2
                for tt in range(NT):
                    trow = t0 + tt * 128
                    b.dma("s", AR_U[:, 0:512], sX1[trow:trow + 128, os_ * 512:(os_ + 1) * 512],
                          "AR_U", "AR_U")
                    b.stage("v", (PBN[os2 * 4 + tt], "AR_U"), ("AR_U2",),
                            ("tensor_tensor", AR_U[:, 512:1024], PB[os2 * 4 + tt][:, 0:512],
                             AR_U[:, 0:512], ALU.add))
                    b.dma("y", out[trow:trow + 128, os_ * 512:(os_ + 1) * 512],
                          AR_U[:, 512:1024], "AR_U2", None)

    b.begin("v", (), ("statfin",))
    b.op("memset", stat[:, 7:8], 0.0)
    tok_fin = b.end()
    for qd in DQS:
        if b.dcnt[qd] > 0:
            b._raw_wait("v", ("D" + qd, b.dcnt[qd]))
    b.begin("v", ("statfin",), ("statfin",))
    b.op("memset", stat[:, 7:8], 1.0)
    b.end()

    with b.stack:
        esem = {e: b.stack.enter_context(nc.semaphore(f"es_{e}")) for e in ENGS}
        dsem = {q: b.stack.enter_context(nc.semaphore(f"ds_{q}")) for q in DQS}
        blk = b.stack.enter_context(nc.Block())

        def fix(ename):
            items = b.q[ename]

            def run(e):
                last = None
                for item in items:
                    if item[0] == "op":
                        last = item[1](e)
                    elif item[0] == "inc":
                        last.then_inc(esem[ename], 1)
                    elif item[0] == "dinc":
                        last.then_inc(dsem[ename], 16)
                    else:
                        tgt, v = item[1]
                        if tgt.startswith("D"):
                            e.wait_ge(dsem[tgt[1:]], v)
                        else:
                            e.wait_ge(esem[tgt], v)
            return run

        blk.tensor(fix("t"))
        blk.vector(fix("v"))
        blk.scalar(fix("s"))
        blk.gpsimd(fix("g"))
        blk.sync(fix("y"))

    return nc


INPUT_MAP = [
    ("x", "L0_attn_x"), ("Win", "L0_attn_W_in"), ("cw0", "L0_attn_conv_w"),
    ("cb0", "L0_attn_conv_b"), ("Wxp", "L0_attn_W_xproj"), ("Wdt", "L0_attn_W_dt"),
    ("bdt", "L0_attn_b_dt"), ("Alog", "L0_attn_A_log"), ("Dsk", "L0_attn_Dskip"),
    ("Wout", "L0_attn_W_out"), ("Q0w", "L0_ffn_weight"), ("Q0b", "L0_ffn_bias"),
    ("Pj", "L1_attn_in_proj_w"), ("cw1", "L1_attn_conv_w"), ("cb1", "L1_attn_conv_b"),
    ("dtb1", "L1_attn_dt_bias"), ("Alog1", "L1_attn_A_log"), ("D1", "L1_attn_D"),
    ("nw1", "L1_attn_norm_w"), ("Op", "L1_attn_out_proj_w"), ("Up", "L1_ffn_up_weight"),
    ("Dn", "L1_ffn_down_weight"), ("ups", "L1_ffn_up_scale"), ("dns", "L1_ffn_down_scale"),
    ("oss", "L1_ffn_output_scale"),
]


def make_in_maps(inputs, ncores=NCORES):
    in_maps = []
    for core in range(ncores):
        mm = {}
        for k, src in INPUT_MAP:
            a = np.ascontiguousarray(np.asarray(inputs[src], dtype=np.float32))
            if k == "x":
                a = np.ascontiguousarray(a[core])
            mm[k] = a
        in_maps.append(mm)
    return in_maps


def kernel(**inputs):
    nc = build_core_program()
    in_maps = make_in_maps(inputs)
    res = run_bass_kernel_spmd(nc, in_maps, core_ids=list(range(NCORES)))
    outs = [np.asarray(r["out"], dtype=np.float32) for r in res.results]
    return np.stack(outs, axis=0)
